# version 22
# baseline (speedup 1.0000x reference)
import numpy as np
import contextlib
import concourse.bass as bass
import concourse.mybir as mybir
from concourse.bass_utils import run_bass_kernel_spmd

F32 = mybir.dt.float32
BF16 = mybir.dt.bfloat16
AF = mybir.ActivationFunctionType
ALU = mybir.AluOpType
AX = mybir.AxisListType

D = 2048
DK = 2112
TOK = 2048
NP = 4
NTB = 16
SEQ = 8192
NJ = 64
SHIFT = 25.0
EPS = 1e-5
INV_D = 1.0 / 2048.0

ENGS = ("pe", "dve", "act", "pool", "sync")


class Plan:

    def __init__(self):
        self.ops = {e: [] for e in ENGS}
        self.cnt = {"pe": 0, "dve": 0, "act": 0, "pool": 0}
        self.dma_sems = []
        self.cc_sems = []

    def _comp(self, eng, sem, fn, waits, serialize):
        w = list(waits)
        if serialize and self.cnt[sem] > 0:
            w.append((sem, self.cnt[sem]))
        self.cnt[sem] += 1
        self.ops[eng].append((w, fn, (sem, 1)))
        return self.cnt[sem]

    def pe(self, fn, waits=()):
        return self._comp("pe", "pe", fn, waits, serialize=False)

    def dve(self, fn, waits=()):
        return self._comp("dve", "dve", fn, waits, serialize=True)

    def act(self, fn, waits=()):
        return self._comp("act", "act", fn, waits, serialize=True)

    def poolc(self, fn, waits=()):
        return self._comp("pool", "pool", fn, waits, serialize=True)

    def _dma(self, eng, sem, fn, waits):
        if sem not in self.cnt:
            self.cnt[sem] = 0
            self.dma_sems.append(sem)
        self.cnt[sem] += 16
        self.ops[eng].append((list(waits), fn, (sem, 16)))
        return self.cnt[sem]

    def pdma(self, fn, sem, waits=()):
        return self._dma("pool", sem, fn, waits)

    def sdma(self, fn, sem, waits=()):
        return self._dma("sync", sem, fn, waits)

    def adma(self, fn, sem, waits=()):
        return self._dma("act", sem, fn, waits)

    def cc(self, fn, name, waits=()):
        self.cc_sems.append(name)
        self.ops["pool"].append((list(waits), fn, (name, 1)))

    def wait_only(self, eng, waits):
        self.ops[eng].append((list(waits), None, None))


def build(**bass_kwargs):
    nc = bass.Bass("TRN2", target_bir_lowering=False, debug=False, **bass_kwargs)

    x_tok = nc.dram_tensor("x_tok", [TOK, D], F32, kind="ExternalInput").ap()
    xT = nc.dram_tensor("xT", [D, TOK], F32, kind="ExternalInput").ap()
    wqT = nc.dram_tensor("wqT", [D, 64], F32, kind="ExternalInput").ap()
    wkT = nc.dram_tensor("wkT", [D, 64], F32, kind="ExternalInput").ap()
    wvT_f = nc.dram_tensor("wvT_f", [D, D], F32, kind="ExternalInput").ap()
    woT_f = nc.dram_tensor("woT_f", [D, D], F32, kind="ExternalInput").ap()
    wbT_f = nc.dram_tensor("wbT_f", [D, D], F32, kind="ExternalInput").ap()
    wrT = nc.dram_tensor("wrT", [D, 8], F32, kind="ExternalInput").ap()
    aT = nc.dram_tensor("aT", [D, 64], F32, kind="ExternalInput").ap()
    bTc = nc.dram_tensor("bTc", [64, D], F32, kind="ExternalInput").ap()
    bbr = nc.dram_tensor("bbr", [1, D], F32, kind="ExternalInput").ap()
    y_out = nc.dram_tensor("y_out", [TOK, D], F32, kind="ExternalOutput").ap()

    agv_in = nc.dram_tensor("agv_in", [TOK, DK], BF16)
    agv_out = [nc.dram_tensor(f"agv_out{t}", [512, DK], BF16) for t in range(NTB)]

    BASE = 18432
    off = [0]

    SB_OFF = {}

    def sb(name, shape, dtype, at=None):
        o = at if at is not None else off[0]
        SB_OFF[name] = o
        t = nc.alloc_sbuf_tensor_at(name, shape, dtype, offset=BASE + o)
        if at is None:
            nb = shape[-1] * (2 if dtype == BF16 else 4)
            off[0] = o + ((nb + 31) // 32) * 32
        return t

    WVT = sb("WVT", [128, 16 * 2048], BF16, at=0)
    PT = sb("PT", [128, 64 * 512], BF16, at=0)
    WS = [sb("WS0", [128, 16 * 512], BF16, at=0),
          sb("WS1", [128, 16 * 512], BF16, at=16384)]
    TPOSE = sb("TPOSE", [128, 64 * 128], BF16, at=32768)
    X1B = [sb("X1B0", [128, 2048], BF16, at=49152),
           sb("X1B1", [128, 2048], BF16, at=53248)]
    off[0] = 65536
    XF32 = sb("XF32", [128, 4 * 2048], F32)
    KFT = sb("KFT", [64, SEQ], BF16)
    OTBUF = sb("OTBUF", [128, 16 * 512], BF16)
    VS = sb("VS", [128, 16 * 512], BF16)
    QFT = sb("QFT", [64, TOK], BF16)
    XTT = [sb(f"XTT{i}", [128, 16 * 128], BF16) for i in range(4)]
    WSP = sb("WSP", [128, 16 * 512], BF16, at=SB_OFF["XTT0"])
    VOWN = [sb("VOWN0", [128, DK], BF16), sb("VOWN1", [128, DK], BF16)]
    KTMP = sb("KTMP", [128, 64 * 64], BF16)
    QTMP = sb("QTMP", [128, 64], BF16)
    WQ = sb("WQ", [128, 16 * 64], BF16)
    WK = sb("WK", [128, 16 * 64], BF16)
    WR = sb("WR", [128, 16 * 8], BF16)
    AT = sb("AT", [128, 16 * 64], BF16)
    BT = sb("BT", [64, D], BF16)
    BB = sb("BB", [1, D], BF16)
    ONESR = sb("ONESR", [1, 128], BF16)
    ONESC = sb("ONESC", [128, 1], BF16)
    ONEF = sb("ONEF", [1, 1], F32)
    IDB = sb("IDB", [128, 128], BF16)
    SHIFT_SB = sb("SHIFT_SB", [128, 1], F32)
    DEPS_SB = sb("DEPS_SB", [128, 1], F32)
    LND2_SB = sb("LND2_SB", [128, 1], F32)
    NH_SB = sb("NH_SB", [128, 1], F32)
    NMR4 = sb("NMR4", [128, 4], F32)
    MNV4 = sb("MNV4", [128, 4], F32)
    RSTD4 = sb("RSTD4", [128, 4], F32)
    MEAN4 = sb("MEAN4", [128, 4], F32)
    VAR4 = sb("VAR4", [128, 4], F32)
    MSQ4 = sb("MSQ4", [128, 4], F32)
    LROW = sb("LROW", [1, 512], F32)
    LC = sb("LC", [128, 4], F32)
    RL = sb("RL", [128, 4], F32)
    LG = sb("LG", [128, 8], F32)
    T8 = sb("T8", [128, 8], F32)
    M2B = sb("M2B", [128, 8], F32)
    MSK = sb("MSK", [128, 8], F32)
    WG = sb("WG", [128, 8], F32)
    EG = sb("EG", [128, 8], F32)
    NM1 = sb("NM1", [128, 1], F32)
    DEN = sb("DEN", [128, 1], F32)
    RD = sb("RD", [128, 1], F32)
    H = sb("H", [128, 64], F32)
    HP = sb("HP", [128, 64], BF16)
    HT = sb("HT", [64, 512], BF16)
    MEANP = sb("MEANP", [128, 16], F32)
    SQP = sb("SQP", [128, 16], F32)
    MEAN = sb("MEAN", [128, 1], F32)
    MSQ = sb("MSQ", [128, 1], F32)
    VAR = sb("VAR", [128, 1], F32)
    RSTD = sb("RSTD", [128, 1], F32)
    SQS = sb("SQS", [128, 512], F32)
    assert off[0] <= 208 * 1024, off[0]

    PSA = [nc.alloc_psum_tensor(f"PSA{i}", [128, 512], F32) for i in range(4)]
    PSS = nc.alloc_psum_tensor("PSS", [128, 512], F32)
    PSL = nc.alloc_psum_tensor("PSL", [128, 512], F32)
    PST = [nc.alloc_psum_tensor(f"PST{i}", [128, 1024], BF16) for i in range(2)]

    P = Plan()

    P.poolc(lambda g: g.memset(ONESR[:, :], 1.0))
    P.poolc(lambda g: g.memset(ONESC[:, :], 1.0))
    P.poolc(lambda g: g.memset(ONEF[:, :], 1.0))
    import math
    P.poolc(lambda g: g.memset(SHIFT_SB[:, :], -SHIFT))
    P.poolc(lambda g: g.memset(DEPS_SB[:, :], 2048.0 * EPS))
    P.poolc(lambda g: g.memset(LND2_SB[:, :], 0.5 * math.log(2048.0)))
    P.poolc(lambda g: g.memset(NH_SB[:, :], -0.5))
    P.poolc(lambda g: g.memset(IDB[:, :], 0.0))
    v_memset = P.poolc(lambda g: g.affine_select(
        out=IDB[:, :], in_=IDB[:, :], compare_op=ALU.not_equal,
        fill=1.0, base=0, pattern=[[-1, 128]], channel_multiplier=1))

    def chunked(dst_sb, src_dram):
        return lambda g: g.dma_start(
            out=dst_sb[:, :].rearrange("p (c w) -> p c w", c=16),
            in_=src_dram.rearrange("(c p) w -> p c w", c=16))

    P.pdma(chunked(WQ, wqT), "dW")
    P.pdma(chunked(WK, wkT), "dW")
    P.pdma(chunked(WR, wrT), "dW")
    P.pdma(chunked(AT, aT), "dW")
    P.pdma(lambda g: g.dma_start(out=BT[:, :], in_=bTc), "dW")
    P.pdma(lambda g: g.dma_start(out=BB[:, :], in_=bbr), "dW")
    v_w = 0
    for c4 in range(4):
        v_w = P.pdma(lambda g, c4=c4: g.dma_start(
            out=WVT[:, :].rearrange("p (c w) -> p c w", c=16)[:, 4 * c4:4 * c4 + 4, :],
            in_=wvT_f.rearrange("(c p) w -> p c w", c=16)[:, 4 * c4:4 * c4 + 4, :]), "dW")

    groups = [[0, 1, 2, 3], [4, 5, 6, 7]]
    xtt_load_done = {}
    xtt_consumed = {}
    evac_done = {}
    vown_store_done = {}

    def emit_xtt_load(tb):
        w = [("pe", xtt_consumed[tb - 4])] if tb >= 4 else []
        xtt_load_done[tb] = P.pdma(
            lambda g, tb=tb: g.dma_start(
                out=XTT[tb % 4][:, :].rearrange("p (c t) -> p c t", c=16),
                in_=xT.rearrange("(c p) t -> p c t", c=16)[:, :, tb * 128:(tb + 1) * 128]),
            f"dX{tb % 4}", waits=w)

    for _tb in range(4):
        emit_xtt_load(_tb)

    qe_val = {}
    for tb in range(NTB):
        buf = XTT[tb % 4]
        w = [(f"dX{tb % 4}", xtt_load_done[tb]), ("dW", v_w), ("pool", v_memset)]
        if tb >= 1:
            w.append(("dve", qe_val[tb - 1]))
        first = True
        for dc in range(16):
            tile = buf[:, dc * 128:(dc + 1) * 128]
            P.pe(lambda t, tile=tile, dc=dc: t.matmul(
                PSS[:, 0:64], tile, WQ[:, dc * 64:(dc + 1) * 64],
                start=(dc == 0), stop=(dc == 15)),
                waits=w if first else ())
            first = False
            P.pe(lambda t, tile=tile, dc=dc: t.matmul(
                PSL[:, 0:64], tile, WK[:, dc * 64:(dc + 1) * 64],
                start=(dc == 0), stop=(dc == 15)))
            for oc in range(4):
                P.pe(lambda t, tile=tile, oc=oc, dc=dc: t.matmul(
                    PSA[oc][:, 0:512], tile,
                    WVT[:, dc * 2048 + oc * 512: dc * 2048 + (oc + 1) * 512],
                    start=(dc == 0), stop=(dc == 15)))
        pe_mm = P.cnt["pe"]
        xtt_consumed[tb] = pe_mm
        if tb + 4 < NTB:
            emit_xtt_load(tb + 4)

        if tb >= 1:
            tpd = P.pe(lambda t: t.transpose(PST[0][0:64, 0:128], QTMP[:, 0:64],
                                             IDB[:, :]),
                       waits=[("dve", qe)])
            evac_done[tb - 1] = P.dve(lambda v, tb=tb: v.tensor_copy(
                QFT[:, (tb - 1) * 128: tb * 128], PST[0][0:64, 0:128]),
                waits=[("pe", tpd)])
        w2 = [("pe", pe_mm)]
        if tb >= 2:
            w2.append((f"dVO{tb % 2}", vown_store_done[tb - 2]))
        for oc in range(4):
            P.dve(lambda v, tb=tb, oc=oc: v.tensor_copy(
                VOWN[tb % 2][:, oc * 512:(oc + 1) * 512], PSA[oc][:, 0:512]),
                waits=w2 if oc == 0 else ())
        P.dve(lambda v, tb=tb: v.tensor_copy(VOWN[tb % 2][:, 2048:2112], PSL[:, 0:64]))
        qe = qe_val[tb] = P.dve(lambda v: v.tensor_copy(QTMP[:, 0:64], PSS[:, 0:64]))
        vown_store_done[tb] = P.sdma(
            lambda s, tb=tb: s.dma_start(
                out=agv_in.ap()[tb * 128:(tb + 1) * 128, :], in_=VOWN[tb % 2][:, :]),
            f"dVO{tb % 2}", waits=[("dve", qe)])

        P.cc(lambda g, tb=tb: g.collective_compute(
            "AllGather", ALU.bypass, replica_groups=groups,
            ins=[agv_in.ap()[tb * 128:(tb + 1) * 128, :]], outs=[agv_out[tb].ap()]),
            f"agv{tb}", waits=[(f"dVO{tb % 2}", vown_store_done[tb])])

    tpd = P.pe(lambda t: t.transpose(PST[0][0:64, 0:128], QTMP[:, 0:64], IDB[:, :]),
               waits=[("dve", qe_val[NTB - 1])])
    evac_done[NTB - 1] = P.dve(lambda v: v.tensor_copy(
        QFT[:, (NTB - 1) * 128: NTB * 128], PST[0][0:64, 0:128]),
        waits=[("pe", tpd)])


    kft_evac = {}
    ktmp_loaded = {}

    def emit_kft_load(k):
        for i in range(4):
            t = 4 * k + i
            ktmp_loaded[(k, i)] = P.sdma(
                lambda s, k=k, i=i, t=t: s.dma_start(
                    out=KTMP[:, 64 * (16 * k + 4 * i): 64 * (16 * k + 4 * i + 4)]
                    .rearrange("p (n w) -> p n w", n=4),
                    in_=agv_out[t].ap().rearrange("(n p) w -> p n w", n=4)[:, :, 2048:2112]),
                f"dKT{k}", waits=[(f"agv{t}", 1)])

    def emit_kft_chunk(k):
        for n in range(16):
            jj = 16 * k + n
            w = [(f"dKT{k}", ktmp_loaded[(k, 3)])]
            if jj >= 2:
                w.append(("dve", kft_evac[jj - 2]))
            if jj == 0:
                w.append(("dve", evac_done[NTB - 1]))
            tp = P.pe(lambda t, jj=jj: t.transpose(
                PST[jj % 2][0:64, 0:128], KTMP[:, jj * 64:(jj + 1) * 64], IDB[:, :]),
                waits=w)
            kft_evac[jj] = P.dve(lambda v, jj=jj: v.tensor_copy(
                KFT[:, jj * 128:(jj + 1) * 128], PST[jj % 2][0:64, 0:128]),
                waits=[("pe", tp)])

    emit_kft_load(0)
    emit_kft_chunk(0)

    def emit_ws_load(oc, wsrc, extra=()):
        return P.pdma(lambda g, oc=oc, wsrc=wsrc: g.dma_start(
            out=WS[oc % 2][:, :].rearrange("p (c w) -> p c w", c=16),
            in_=wsrc.rearrange("(c p) w -> p c w", c=16)[:, :, oc * 512:(oc + 1) * 512]),
            f"dWS{oc % 2}", waits=list(extra))

    out_store_total = {}
    ph45_pe_done = 0
    vload_consumed_prev = None
    sq_done = {}
    prev_stt_gate = 0
    pf_cur = {}

    for p in range(NP):
        dve_barrier = prev_stt_gate if p >= 1 else P.cnt["dve"]
        pass_pe_barrier = ph45_pe_done
        exp_done = {}
        st_done = {}
        for jj_, v_ in pf_cur.items():
            st_done[jj_], exp_done[jj_] = v_
        pf_next = {}
        acc_evac = {}
        vload = {}
        vload_consumed = {}
        sweep_mid_pe = {}
        sweep_end_pe = {}

        for s_ in range(4):
            def emit_vload(bi, s_=s_, p=p):
                w = [(f"agv{bi}", 1)]
                if bi >= 4:
                    w.append(("pe", vload_consumed[(s_, bi - 4)]))
                elif s_ >= 1:
                    w.append(("pe", vload_consumed[(s_ - 1, 12 + bi)]))
                elif p >= 1:
                    w.append(("pe", vload_consumed_prev[(3, 12 + bi)]))
                emit = P.sdma if (bi % 2 == 0 or (s_ == 0 and bi < 4)) else P.pdma
                vload[(s_, bi)] = emit(
                    lambda e, bi=bi, s_=s_: e.dma_start(
                        out=VS[:, :].rearrange("p (n w) -> p n w", n=16)[
                            :, (bi % 4) * 4:(bi % 4) * 4 + 4, :],
                        in_=agv_out[bi].ap().rearrange("(n p) w -> p n w", n=4)[
                            :, :, s_ * 512:(s_ + 1) * 512]),
                    f"dVS{bi % 4}", waits=w)

            for bb_ in range(4):
                emit_vload(bb_)

            if s_ == 0:
                def st_bank(jj, p=p):
                    if p == 0 or jj % 2 == 0:
                        return PSS[:, 0:512]
                    return PST[0][:, :].bitcast(F32)[:, 0:512]
                if 0 not in st_done:
                    w0 = [("dve", kft_evac[0]),
                          ("dve", dve_barrier), ("pe", pass_pe_barrier)]
                    st_done[0] = P.pe(lambda t, p=p: t.matmul(
                        st_bank(0), KFT[:, 0:128],
                        QFT[:, p * 512:(p + 1) * 512], start=True, stop=True),
                        waits=w0)

            for bi in range(16):
                if p == 0 and s_ == 1 and bi == 1:
                    wsp_load = P.pdma(lambda g: g.dma_start(
                        out=WSP[:, :].rearrange("p (c w) -> p c w", c=16),
                        in_=woT_f.rearrange("(c p) w -> p c w", c=16)[:, :, 0:512]),
                        "dWSP", waits=[("pe", xtt_consumed[NTB - 1])])
                if s_ == 1 and bi == 4:
                    lcol_v = 0
                    for qb in range(4):
                        lcol_v = P.pe(lambda t, qb=qb: t.matmul(
                            PSS[:, qb:qb + 1], LROW[0:1, qb * 128:(qb + 1) * 128],
                            ONEF[0:1, 0:1], start=True, stop=True),
                            waits=[("dve", lrow_v)])
                    P.dve(lambda v: v.tensor_copy(LC[:, 0:4], PSS[:, 0:4]),
                          waits=[("pe", lcol_v)])
                    P.dve(lambda v: v.reciprocal(RL[:, 0:4], LC[:, 0:4]))
                if p == 0 and s_ == 0 and bi >= 4 and bi % 4 == 0:
                    k = bi // 4
                    emit_kft_load(k)
                    emit_kft_chunk(k)
                    jj0 = 16 * k
                    st_done[jj0] = P.pe(lambda t, jj0=jj0, p=p: t.matmul(
                        PSS[:, 0:512], KFT[:, jj0 * 128:(jj0 + 1) * 128],
                        QFT[:, p * 512:(p + 1) * 512], start=True, stop=True),
                        waits=[("act", exp_done[jj0 - 1]), ("dve", kft_evac[jj0])])
                pv_v = 0
                for j8 in range(4):
                    jj = 4 * bi + j8
                    slot = (bi % 4) * 4 + j8
                    if s_ == 0:
                        if jj not in exp_done:
                            exp_done[jj] = P.act(lambda s, jj=jj: s.activation(
                                PT[:, jj * 512:(jj + 1) * 512], st_bank(jj), AF.Exp,
                                bias=SHIFT_SB[:, 0:1]),
                                waits=[("pe", st_done[jj]), ("pool", v_memset)])
                        nxt = jj + 1
                        defer = (p == 0 and nxt % 16 == 0)
                        if nxt < NJ and not defer and nxt not in st_done:
                            eg = exp_done[jj] if (p == 0) else exp_done.get(nxt - 2, 0)
                            wn = [("act", eg), ("dve", kft_evac[nxt])]
                            if p > 0 and nxt % 2 == 1 and nxt == 1:
                                wn.append(("dve", dve_barrier))
                            st_done[nxt] = P.pe(lambda t, nxt=nxt, p=p: t.matmul(
                                st_bank(nxt), KFT[:, nxt * 128:(nxt + 1) * 128],
                                QFT[:, p * 512:(p + 1) * 512], start=True, stop=True),
                                waits=wn)
                    w = [(f"dVS{bi % 4}", vload[(s_, bi)]), ("act", exp_done[jj])]
                    if jj == 0 and s_ == 0:
                        w.append(("dve", dve_barrier))
                    for i in range(4):
                        wi = w if i == 0 else []
                        if jj == 0 and s_ >= 1:
                            wi = list(wi) + [acc_evac[i]]
                        pv_v = P.pe(lambda t, jj=jj, i=i, slot=slot: t.matmul(
                            PSA[i][:, 0:512],
                            VS[:, slot * 512 + i * 128: slot * 512 + (i + 1) * 128],
                            PT[:, jj * 512:(jj + 1) * 512],
                            start=(jj == 0), stop=(jj == 63)),
                            waits=wi)
                    if s_ == 0:
                        pv_v = P.pe(lambda t, jj=jj: t.matmul(
                            PSL[0:1, 0:512], ONESC[:, 0:1],
                            PT[:, jj * 512:(jj + 1) * 512],
                            start=(jj == 0), stop=(jj == 63)))
                    if jj == 31:
                        sweep_mid_pe[s_] = pv_v
                vload_consumed[(s_, bi)] = pv_v
                if bi + 4 < 16:
                    emit_vload(bi + 4)
            sweep_end_pe[s_] = pv_v

            ev = {}
            for i in range(4):
                dcg = 4 * s_ + i
                if i % 2 == 0:
                    ev[i] = ("dve", P.dve(lambda v, i=i, dcg=dcg: v.tensor_copy(
                        OTBUF[:, dcg * 512:(dcg + 1) * 512], PSA[i][:, 0:512]),
                        waits=[("pe", pv_v)] if i == 0 else ()))
                else:
                    ev[i] = ("act", P.act(lambda s, i=i, dcg=dcg: s.activation(
                        OTBUF[:, dcg * 512:(dcg + 1) * 512], PSA[i][:, 0:512],
                        AF.Identity),
                        waits=[("pe", pv_v)] if i == 1 else ()))
            acc_evac = ev
            if s_ == 0:
                lrow_v = P.dve(lambda v: v.tensor_copy(LROW[0:1, 0:512], PSL[0:1, 0:512]),
                               waits=[("pe", pv_v)])
        vload_consumed_prev = vload_consumed
        ot_waits = list(acc_evac.values())

        def stats_dve(tb):
            P.dve(lambda v, tb=tb: v.reduce_sum(
                MEAN4[:, tb:tb + 1], MEANP[:, tb * 4:tb * 4 + 4], axis=AX.X))
            P.dve(lambda v, tb=tb: v.tensor_scalar(
                VAR4[:, tb:tb + 1], MEAN4[:, tb:tb + 1], MEAN4[:, tb:tb + 1],
                -INV_D, ALU.mult, ALU.mult))
            P.dve(lambda v, tb=tb: v.tensor_scalar_add(
                VAR4[:, tb:tb + 1], VAR4[:, tb:tb + 1], 2048.0 * EPS))
            return P.dve(lambda v, tb=tb: v.tensor_scalar_mul(
                MNV4[:, tb:tb + 1], MEAN4[:, tb:tb + 1], INV_D))

        def stats_act(tb, dvegate):
            P.act(lambda s, tb=tb: s.activation(
                SQS[:, 0:4], SQP[:, tb * 4:tb * 4 + 4], AF.Identity,
                accum_out=MSQ4[:, tb:tb + 1]), waits=[("dve", dvegate)])
            P.act(lambda s, tb=tb: s.activation(
                VAR4[:, tb:tb + 1], MSQ4[:, tb:tb + 1], AF.Ln,
                bias=VAR4[:, tb:tb + 1]))
            return P.act(lambda s, tb=tb: s.activation(
                RSTD4[:, tb:tb + 1], VAR4[:, tb:tb + 1], AF.Exp,
                scale=NH_SB[:, 0:1], bias=LND2_SB[:, 0:1]))

        def emit_nm(tb, rs):
            return P.dve(lambda v, tb=tb: v.tensor_scalar(
                NMR4[:, tb:tb + 1], RSTD4[:, tb:tb + 1], MEAN4[:, tb:tb + 1],
                -INV_D, ALU.mult, ALU.mult), waits=[("act", rs)])

        def emit_norm_act(tb, extra=()):
            return P.act(lambda s, tb=tb: s.activation(
                XF32[:, tb * 2048:(tb + 1) * 2048],
                XF32[:, tb * 2048:(tb + 1) * 2048], AF.Identity,
                scale=RSTD4[:, tb:tb + 1], bias=NMR4[:, tb:tb + 1]),
                waits=list(extra))

        def emit_cast(tb, extra=()):
            return P.act(lambda s, tb=tb: s.activation(
                X1B[tb % 2][:, 0:2048], XF32[:, tb * 2048:(tb + 1) * 2048], AF.Identity,
                scale=RSTD4[:, tb:tb + 1], bias=NMR4[:, tb:tb + 1]),
                waits=list(extra))

        def emit_norm(tb, extra=()):
            return P.dve(lambda v, tb=tb: v.tensor_scalar(
                XF32[:, tb * 2048:(tb + 1) * 2048], XF32[:, tb * 2048:(tb + 1) * 2048],
                MNV4[:, tb:tb + 1], RSTD4[:, tb:tb + 1], ALU.subtract, ALU.mult),
                waits=list(extra))

        xf_total = 0
        for t4 in range(4):
            w = [("dOUT", out_store_total[p - 1])] if p >= 1 else []
            xf_total = P.sdma(
                lambda s, p=p, t4=t4: s.dma_start(
                    out=XF32[:, t4 * 2048:(t4 + 1) * 2048],
                    in_=x_tok[p * 512 + t4 * 128: p * 512 + (t4 + 1) * 128, :]),
                "dXF", waits=w)
        stt_done = {}
        ln1_norm = {}
        ln1_nm = {}
        cast_done = {}
        rs1 = {}
        ws_load = {1: emit_ws_load(1, woT_f, [("pe", vload_consumed[(3, 7)])]),
                   2: emit_ws_load(2, woT_f, [("pe", vload_consumed[(3, 3)])])}
        for oc in range(4):
            wsrc = WSP if oc == 0 else WS[oc % 2]
            for tb in range(4):
                if oc == 0:
                    w = [("dWSP", 16)] + ot_waits
                else:
                    w = [(f"dWS{oc % 2}", ws_load[oc])] + ot_waits
                    w.append(("dve", stt_done[(oc - 1, tb)]))
                last_mm = 0
                for dc in range(16):
                    last_mm = P.pe(lambda t, tb=tb, dc=dc, oc=oc, wsrc=wsrc: t.matmul(
                        PSA[tb][:, 0:512],
                        OTBUF[:, dc * 512 + tb * 128: dc * 512 + (tb + 1) * 128],
                        wsrc[:, dc * 512:(dc + 1) * 512],
                        start=(dc == 0), stop=(dc == 15)),
                        waits=w if dc == 0 else ())
                sl = slice(tb * 2048 + oc * 512, tb * 2048 + (oc + 1) * 512)
                stt_done[(oc, tb)] = P.dve(
                    lambda v, sl=sl, tb=tb, oc=oc: v.scalar_tensor_tensor(
                        XF32[:, sl], PSA[tb][:, 0:512], RL[:, tb:tb + 1], XF32[:, sl],
                        ALU.mult, ALU.add,
                        accum_out=MEANP[:, tb * 4 + oc: tb * 4 + oc + 1]),
                    waits=[("pe", last_mm), ("dXF", xf_total)])
                sq_done[(oc, tb)] = P.act(
                    lambda s, sl=sl, tb=tb, oc=oc: s.activation(
                        SQS[:, 0:512], XF32[:, sl], AF.Square,
                        accum_out=SQP[:, tb * 4 + oc: tb * 4 + oc + 1]),
                    waits=[("dve", stt_done[(oc, tb)])])
                if oc == 3:
                    rs1[tb] = stats_act(tb, stats_dve(tb))
                    if tb <= 1:
                        ln1_nm[tb] = emit_nm(tb, rs1[tb])
                        cast_done[tb] = emit_cast(tb, [("dve", ln1_nm[tb])])
                        ln1_norm[tb] = emit_norm_act(tb)
            if oc == 1:
                ws_load[3] = emit_ws_load(3, woT_f, [("pe", P.cnt["pe"])])

        ph4_pe = P.cnt["pe"]
        ws5_gate = ph4_pe
        x1t_evac = {}
        route_done = {}
        hp_done = {}
        ht_pe_val = {}

        def emit_ht(tbh):
            ht_pe_val[tbh] = P.pe(lambda t: t.transpose(
                PSL[:, :].bitcast(BF16)[0:64, 0:128], HP[:, 0:64], IDB[:, :]),
                waits=[("dve", hp_done[tbh])])
            route_done[tbh] = P.dve(lambda v, tbh=tbh: v.tensor_copy(
                HT[:, tbh * 128:(tbh + 1) * 128],
                PSL[:, :].bitcast(BF16)[0:64, 0:128]),
                waits=[("pe", ht_pe_val[tbh])])

        for tb in range(4):
            if tb <= 1:
                cast_v = cast_done[tb]
            else:
                ln1_nm[tb] = emit_nm(tb, rs1[tb])
                cast_v = cast_done[tb] = emit_cast(
                    tb, [("dve", ln1_nm[tb]),
                         ("dve", x1t_evac[(tb - 2) * 16 + 15])])
                ln1_norm[tb] = emit_norm_act(tb)
            for dc in range(16):
                i = tb * 16 + dc
                w = [("act", cast_v), ("pe", ph4_pe)]
                if i >= 2:
                    w.append(("dve", x1t_evac[i - 2]))
                P.pe(lambda t, tb=tb, dc=dc, i=i: t.transpose(
                    PST[i % 2][:, 0:128], X1B[tb % 2][:, dc * 128:(dc + 1) * 128],
                    IDB[:, :]),
                    waits=w)
                x1t_evac[i] = P.dve(lambda v, i=i: v.tensor_copy(
                    TPOSE[:, i * 128:(i + 1) * 128], PST[i % 2][:, 0:128]),
                    waits=[("pe", P.cnt["pe"])])
            if tb >= 1:
                emit_ht(tb - 1)
            lg_v = 0
            for dc in range(16):
                lg_v = P.pe(lambda t, tb=tb, dc=dc: t.matmul(
                    PSS[:, 0:8], TPOSE[:, (tb * 16 + dc) * 128:(tb * 16 + dc + 1) * 128],
                    WR[:, dc * 8:(dc + 1) * 8], start=(dc == 0), stop=(dc == 15)),
                    waits=[("dve", x1t_evac[tb * 16 + dc])])
            h_v = 0
            for dc in range(16):
                wh = []
                if dc == 0 and tb >= 1:
                    wh.append(("dve", route_done[tb - 1]))
                h_v = P.pe(lambda t, tb=tb, dc=dc: t.matmul(
                    PSL[:, 0:64], TPOSE[:, (tb * 16 + dc) * 128:(tb * 16 + dc + 1) * 128],
                    AT[:, dc * 64:(dc + 1) * 64], start=(dc == 0), stop=(dc == 15)),
                    waits=wh)
            lg_copy = P.dve(lambda v: v.tensor_copy(LG[:, :], PSS[:, 0:8]),
                            waits=[("pe", lg_v)])
            P.dve(lambda v: v.max(T8[:, :], LG[:, :]))
            P.dve(lambda v: v.tensor_scalar_mul(NM1[:, 0:1], T8[:, 0:1], -1.0))
            P.dve(lambda v: v.tensor_scalar(M2B[:, :], LG[:, :], 0.0, T8[:, 1:2],
                                            ALU.mult, ALU.add))
            msk_v = P.dve(lambda v: v.tensor_tensor(MSK[:, :], LG[:, :], M2B[:, :], ALU.is_ge))
            eg_v = P.act(lambda s: s.activation(EG[:, :], LG[:, :], AF.Exp,
                                                bias=NM1[:, 0:1]), waits=[("dve", msk_v)])
            e2_v = P.act(lambda s: s.activation(DEN[:, 0:1], T8[:, 1:2], AF.Exp,
                                                bias=NM1[:, 0:1]))
            P.dve(lambda v: v.tensor_scalar(DEN[:, 0:1], DEN[:, 0:1], 1.0, 0.5,
                                            ALU.add, ALU.mult), waits=[("act", e2_v)])
            P.dve(lambda v: v.reciprocal(RD[:, 0:1], DEN[:, 0:1]))
            P.dve(lambda v: v.scalar_tensor_tensor(WG[:, :], EG[:, :], RD[:, 0:1],
                                                   MSK[:, :], ALU.mult, ALU.mult),
                  waits=[("act", eg_v)])
            P.dve(lambda v: v.tensor_copy(H[:, :], PSL[:, 0:64]), waits=[("pe", h_v)])
            hp_v = 0
            for e in range(8):
                wh = [("pe", ht_pe_val[tb - 1])] if (e == 0 and tb >= 1) else ()
                hp_v = P.dve(lambda v, e=e: v.tensor_scalar_mul(
                    HP[:, e * 8:(e + 1) * 8], H[:, e * 8:(e + 1) * 8], WG[:, e:e + 1]),
                    waits=wh)
            hp_done[tb] = hp_v
            if tb == 3:
                emit_ht(3)

        ws_load5 = {0: emit_ws_load(0, wbT_f, [("pe", ws5_gate)]),
                    1: emit_ws_load(1, wbT_f, [("pe", ws5_gate)])}
        stt2 = {}
        ln2_norm = {}
        rs2 = {}
        for oc in range(4):
            for tb in range(4):
                w = [(f"dWS{oc % 2}", ws_load5[oc]), ("dve", route_done[tb])]
                if oc >= 1:
                    w.append(("dve", stt2[(oc - 1, tb)]))
                for dc in range(16):
                    P.pe(lambda t, tb=tb, dc=dc, oc=oc: t.matmul(
                        PSA[tb][:, 0:512],
                        TPOSE[:, (tb * 16 + dc) * 128:(tb * 16 + dc + 1) * 128],
                        WS[oc % 2][:, dc * 512:(dc + 1) * 512],
                        start=(dc == 0), stop=False),
                        waits=w if dc == 0 else ())
                P.pe(lambda t, oc=oc, tb=tb: t.matmul(
                    PSA[tb][:, 0:512], ONESR[0:1, 0:128],
                    BB[0:1, oc * 512:(oc + 1) * 512], start=False, stop=False))
                last_mm = P.pe(lambda t, oc=oc, tb=tb: t.matmul(
                    PSA[tb][:, 0:512], HT[:, tb * 128:(tb + 1) * 128],
                    BT[:, oc * 512:(oc + 1) * 512], start=False, stop=True))
                sl = slice(tb * 2048 + oc * 512, tb * 2048 + (oc + 1) * 512)
                wstt = [("pe", last_mm)]
                if oc == 0:
                    wstt.append(("act", ln1_norm[tb]))
                stt2[(oc, tb)] = P.dve(
                    lambda v, sl=sl, tb=tb, oc=oc: v.scalar_tensor_tensor(
                        XF32[:, sl], PSA[tb][:, 0:512], 1.0, XF32[:, sl],
                        ALU.mult, ALU.add,
                        accum_out=MEANP[:, tb * 4 + oc: tb * 4 + oc + 1]),
                    waits=wstt)
                sq_done[(oc, tb)] = P.act(
                    lambda s, sl=sl, tb=tb, oc=oc: s.activation(
                        SQS[:, 0:512], XF32[:, sl], AF.Square,
                        accum_out=SQP[:, tb * 4 + oc: tb * 4 + oc + 1]),
                    waits=[("dve", stt2[(oc, tb)])])
                if oc == 3:
                    rs2[tb] = stats_act(tb, stats_dve(tb))
                if oc == 3 and tb == 1 and p + 1 < NP:
                    st2 = P.pe(lambda t, p=p: t.matmul(
                        PSS[:, 0:512], KFT[:, 256:384],
                        QFT[:, (p + 1) * 512:(p + 2) * 512],
                        start=True, stop=True),
                        waits=[("act", pf_next[0][1])])
                    e2 = P.act(lambda s: s.activation(
                        PT[:, 1024:1536], PSS[:, 0:512], AF.Exp,
                        bias=SHIFT_SB[:, 0:1]), waits=[("pe", st2)])
                    st3 = P.pe(lambda t, p=p: t.matmul(
                        PST[0][:, :].bitcast(F32)[:, 0:512], KFT[:, 384:512],
                        QFT[:, (p + 1) * 512:(p + 2) * 512],
                        start=True, stop=True),
                        waits=[("act", pf_next[1][1])])
                    e3 = P.act(lambda s: s.activation(
                        PT[:, 1536:2048], PST[0][:, :].bitcast(F32)[:, 0:512],
                        AF.Exp, bias=SHIFT_SB[:, 0:1]), waits=[("pe", st3)])
                    pf_next[2] = (st2, e2)
                    pf_next[3] = (st3, e3)
            if oc + 2 < 4:
                ws_load5[oc + 2] = emit_ws_load(oc + 2, wbT_f, [("pe", P.cnt["pe"])])
            if oc == 2 and p + 1 < NP:
                st0 = P.pe(lambda t, p=p: t.matmul(
                    PSS[:, 0:512], KFT[:, 0:128],
                    QFT[:, (p + 1) * 512:(p + 2) * 512], start=True, stop=True),
                    waits=[("dve", lg_copy)])
                e0 = P.act(lambda s: s.activation(
                    PT[:, 0:512], PSS[:, 0:512], AF.Exp, bias=SHIFT_SB[:, 0:1]),
                    waits=[("pe", st0)])
                st1 = P.pe(lambda t, p=p: t.matmul(
                    PST[0][:, :].bitcast(F32)[:, 0:512], KFT[:, 128:256],
                    QFT[:, (p + 1) * 512:(p + 2) * 512], start=True, stop=True),
                    waits=[("dve", x1t_evac[62])])
                e1 = P.act(lambda s: s.activation(
                    PT[:, 512:1024], PST[0][:, :].bitcast(F32)[:, 0:512],
                    AF.Exp, bias=SHIFT_SB[:, 0:1]),
                    waits=[("pe", st1)])
                pf_next = {0: (st0, e0), 1: (st1, e1)}
        prev_stt_gate = stt2[(3, 3)]
        for tb in range(4):
            ln2_norm[tb] = emit_norm(tb, [("act", rs2[tb])])

        tot = 0
        for tb in range(4):
            tot = P.pdma(
                lambda g, p=p, tb=tb: g.dma_start(
                    out=y_out[p * 512 + tb * 128: p * 512 + (tb + 1) * 128, :],
                    in_=XF32[:, tb * 2048:(tb + 1) * 2048]),
                "dOUT", waits=[("dve", ln2_norm[tb])])
        out_store_total[p] = tot
        ph45_pe_done = P.cnt["pe"]
        pf_cur = pf_next

    P.wait_only("pool", [("dOUT", P.cnt["dOUT"])])

    with contextlib.ExitStack() as stack:
        sems = {}
        for name in ["pe", "dve", "act", "pool"] + P.dma_sems + P.cc_sems:
            sems[name] = stack.enter_context(nc.semaphore(f"sem_{name}"))
        block = stack.enter_context(nc.Block())

        def run_stream(eng_obj, entries):
            last_waited = {}
            for waits, fn, inc in entries:
                for sname, val in waits:
                    if val <= 0 or last_waited.get(sname, -1) >= val:
                        continue
                    eng_obj.wait_ge(sems[sname], val)
                    last_waited[sname] = val
                if fn is None:
                    continue
                inst = fn(eng_obj)
                if inc is not None:
                    inst.then_inc(sems[inc[0]], inc[1])

        @block.tensor
        def _(t):
            run_stream(t, P.ops["pe"])

        @block.vector
        def _(v):
            run_stream(v, P.ops["dve"])

        @block.scalar
        def _(s):
            run_stream(s, P.ops["act"])

        @block.gpsimd
        def _(g):
            run_stream(g, P.ops["pool"])

        @block.sync
        def _(s):
            run_stream(s, P.ops["sync"])

    return nc


_NC_CACHE = {}


def _get_nc():
    if "nc" not in _NC_CACHE:
        _NC_CACHE["nc"] = build()
    return _NC_CACHE["nc"]


def kernel(L0_attn_hidden_states, L0_attn_Wq, L0_attn_Wk, L0_attn_Wv, L0_attn_Wo,
           L0_ffn_W_base, L0_ffn_b_base, L0_ffn_W_router, L0_ffn_A_w, L0_ffn_B_w):
    x = np.ascontiguousarray(np.asarray(L0_attn_hidden_states, dtype=np.float32)).reshape(16384, 2048)
    wq = np.asarray(L0_attn_Wq, dtype=np.float32)
    wk = np.asarray(L0_attn_Wk, dtype=np.float32)
    wv = np.asarray(L0_attn_Wv, dtype=np.float32)
    wo = np.asarray(L0_attn_Wo, dtype=np.float32)
    wb = np.asarray(L0_ffn_W_base, dtype=np.float32)
    bb = np.asarray(L0_ffn_b_base, dtype=np.float32)
    wr = np.asarray(L0_ffn_W_router, dtype=np.float32)
    aw = np.asarray(L0_ffn_A_w, dtype=np.float32)
    bw = np.asarray(L0_ffn_B_w, dtype=np.float32)

    shared = {
        "wqT": np.ascontiguousarray(wq.T),
        "wkT": np.ascontiguousarray(wk.T),
        "wvT_f": np.ascontiguousarray(wv.T),
        "woT_f": np.ascontiguousarray(wo.T),
        "wbT_f": np.ascontiguousarray(wb.T),
        "wrT": np.ascontiguousarray(wr.T),
        "aT": np.ascontiguousarray(aw.transpose(2, 0, 1).reshape(2048, 64)),
        "bTc": np.ascontiguousarray(bw.transpose(0, 2, 1).reshape(64, 2048)),
        "bbr": np.ascontiguousarray(bb.reshape(1, 2048)),
    }

    nc = _get_nc()
    in_maps = []
    for c in range(8):
        sl = x[2048 * c:2048 * (c + 1)]
        m = dict(shared)
        m["x_tok"] = np.ascontiguousarray(sl)
        m["xT"] = np.ascontiguousarray(sl.T)
        in_maps.append(m)

    res = run_bass_kernel_spmd(nc, in_maps, core_ids=list(range(8)))
    out = np.concatenate([np.asarray(res.results[c]["y_out"]) for c in range(8)], axis=0)
    return out.reshape(2, 8192, 2048).astype(np.float32, copy=False)


# version 23
# speedup vs baseline: 1.0360x; 1.0360x over previous
import numpy as np
import contextlib
import concourse.bass as bass
import concourse.mybir as mybir
from concourse.bass_utils import run_bass_kernel_spmd

F32 = mybir.dt.float32
BF16 = mybir.dt.bfloat16
AF = mybir.ActivationFunctionType
ALU = mybir.AluOpType
AX = mybir.AxisListType

D = 2048
DK = 2112
TOK = 2048
NP = 4
NTB = 16
SEQ = 8192
NJ = 64
SHIFT = 25.0
EPS = 1e-5
INV_D = 1.0 / 2048.0

ENGS = ("pe", "dve", "act", "pool", "sync")


class Plan:

    def __init__(self):
        self.ops = {e: [] for e in ENGS}
        self.cnt = {"pe": 0, "dve": 0, "act": 0, "pool": 0}
        self.dma_sems = []
        self.cc_sems = []

    def _comp(self, eng, sem, fn, waits, serialize):
        w = list(waits)
        if serialize and self.cnt[sem] > 0:
            w.append((sem, self.cnt[sem]))
        self.cnt[sem] += 1
        self.ops[eng].append((w, fn, (sem, 1)))
        return self.cnt[sem]

    def pe(self, fn, waits=()):
        return self._comp("pe", "pe", fn, waits, serialize=False)

    def dve(self, fn, waits=()):
        return self._comp("dve", "dve", fn, waits, serialize=True)

    def act(self, fn, waits=()):
        return self._comp("act", "act", fn, waits, serialize=True)

    def poolc(self, fn, waits=()):
        return self._comp("pool", "pool", fn, waits, serialize=True)

    def _dma(self, eng, sem, fn, waits):
        if sem not in self.cnt:
            self.cnt[sem] = 0
            self.dma_sems.append(sem)
        self.cnt[sem] += 16
        self.ops[eng].append((list(waits), fn, (sem, 16)))
        return self.cnt[sem]

    def pdma(self, fn, sem, waits=()):
        return self._dma("pool", sem, fn, waits)

    def sdma(self, fn, sem, waits=()):
        return self._dma("sync", sem, fn, waits)

    def adma(self, fn, sem, waits=()):
        return self._dma("act", sem, fn, waits)

    def cc(self, fn, name, waits=()):
        self.cc_sems.append(name)
        self.ops["pool"].append((list(waits), fn, (name, 1)))

    def wait_only(self, eng, waits):
        self.ops[eng].append((list(waits), None, None))


def build(**bass_kwargs):
    nc = bass.Bass("TRN2", target_bir_lowering=False, debug=False, **bass_kwargs)

    x_tok = nc.dram_tensor("x_tok", [TOK, D], F32, kind="ExternalInput").ap()
    xT = nc.dram_tensor("xT", [D, TOK], F32, kind="ExternalInput").ap()
    wqT = nc.dram_tensor("wqT", [D, 64], F32, kind="ExternalInput").ap()
    wkT = nc.dram_tensor("wkT", [D, 64], F32, kind="ExternalInput").ap()
    wvT_f = nc.dram_tensor("wvT_f", [D, D], F32, kind="ExternalInput").ap()
    woT_f = nc.dram_tensor("woT_f", [D, D], F32, kind="ExternalInput").ap()
    wbT_f = nc.dram_tensor("wbT_f", [D, D], F32, kind="ExternalInput").ap()
    wrT = nc.dram_tensor("wrT", [D, 8], F32, kind="ExternalInput").ap()
    aT = nc.dram_tensor("aT", [D, 64], F32, kind="ExternalInput").ap()
    bTc = nc.dram_tensor("bTc", [64, D], F32, kind="ExternalInput").ap()
    bbr = nc.dram_tensor("bbr", [1, D], F32, kind="ExternalInput").ap()
    y_out = nc.dram_tensor("y_out", [TOK, D], F32, kind="ExternalOutput").ap()

    agv_in = nc.dram_tensor("agv_in", [TOK, DK], BF16)
    agv_out = [nc.dram_tensor(f"agv_out{t}", [512, DK], BF16) for t in range(NTB)]

    BASE = 18432
    off = [0]

    SB_OFF = {}

    def sb(name, shape, dtype, at=None):
        o = at if at is not None else off[0]
        SB_OFF[name] = o
        t = nc.alloc_sbuf_tensor_at(name, shape, dtype, offset=BASE + o)
        if at is None:
            nb = shape[-1] * (2 if dtype == BF16 else 4)
            off[0] = o + ((nb + 31) // 32) * 32
        return t

    WVT = sb("WVT", [128, 16 * 2048], BF16, at=0)
    PT = sb("PT", [128, 64 * 512], BF16, at=0)
    WS = [sb("WS0", [128, 16 * 512], BF16, at=0),
          sb("WS1", [128, 16 * 512], BF16, at=16384)]
    TPOSE = sb("TPOSE", [128, 64 * 128], BF16, at=32768)
    X1B = [sb("X1B0", [128, 2048], BF16, at=49152),
           sb("X1B1", [128, 2048], BF16, at=53248)]
    off[0] = 65536
    XF32 = sb("XF32", [128, 4 * 2048], F32)
    KFT = sb("KFT", [64, SEQ], BF16)
    OTBUF = sb("OTBUF", [128, 16 * 512], BF16)
    VS = sb("VS", [128, 16 * 512], BF16)
    QFT = sb("QFT", [64, TOK], BF16)
    XTT = [sb(f"XTT{i}", [128, 16 * 128], BF16) for i in range(4)]
    WSP = sb("WSP", [128, 16 * 512], BF16, at=SB_OFF["XTT0"])
    VOWN = [sb("VOWN0", [128, DK], BF16), sb("VOWN1", [128, DK], BF16)]
    KTMP = sb("KTMP", [128, 64 * 64], BF16)
    QTMP = sb("QTMP", [128, 64], BF16)
    WQ = sb("WQ", [128, 16 * 64], BF16)
    WK = sb("WK", [128, 16 * 64], BF16)
    WR = sb("WR", [128, 16 * 8], BF16)
    AT = sb("AT", [128, 16 * 64], BF16)
    BT = sb("BT", [64, D], BF16)
    BB = sb("BB", [1, D], BF16)
    ONESR = sb("ONESR", [1, 128], BF16)
    ONESC = sb("ONESC", [128, 1], BF16)
    ONEF = sb("ONEF", [1, 1], F32)
    IDB = sb("IDB", [128, 128], BF16)
    SHIFT_SB = sb("SHIFT_SB", [128, 1], F32)
    DEPS_SB = sb("DEPS_SB", [128, 1], F32)
    LND2_SB = sb("LND2_SB", [128, 1], F32)
    NH_SB = sb("NH_SB", [128, 1], F32)
    NMR4 = sb("NMR4", [128, 4], F32)
    MNV4 = sb("MNV4", [128, 4], F32)
    RSTD4 = sb("RSTD4", [128, 4], F32)
    MEAN4 = sb("MEAN4", [128, 4], F32)
    VAR4 = sb("VAR4", [128, 4], F32)
    MSQ4 = sb("MSQ4", [128, 4], F32)
    LROW = sb("LROW", [1, 512], F32)
    LC = sb("LC", [128, 4], F32)
    RL = sb("RL", [128, 4], F32)
    LG = sb("LG", [128, 8], F32)
    T8 = sb("T8", [128, 8], F32)
    M2B = sb("M2B", [128, 8], F32)
    MSK = sb("MSK", [128, 8], F32)
    WG = sb("WG", [128, 8], F32)
    EG = sb("EG", [128, 8], F32)
    NM1 = sb("NM1", [128, 1], F32)
    DEN = sb("DEN", [128, 1], F32)
    RD = sb("RD", [128, 1], F32)
    H = sb("H", [128, 64], F32)
    HP = sb("HP", [128, 64], BF16)
    HT = sb("HT", [64, 512], BF16)
    MEANP = sb("MEANP", [128, 16], F32)
    SQP = sb("SQP", [128, 16], F32)
    MEAN = sb("MEAN", [128, 1], F32)
    MSQ = sb("MSQ", [128, 1], F32)
    VAR = sb("VAR", [128, 1], F32)
    RSTD = sb("RSTD", [128, 1], F32)
    SQS = sb("SQS", [128, 512], F32)
    assert off[0] <= 208 * 1024, off[0]

    PSA = [nc.alloc_psum_tensor(f"PSA{i}", [128, 512], F32) for i in range(4)]
    PSS = nc.alloc_psum_tensor("PSS", [128, 512], F32)
    PSL = nc.alloc_psum_tensor("PSL", [128, 512], F32)
    PST = [nc.alloc_psum_tensor(f"PST{i}", [128, 1024], BF16) for i in range(2)]

    P = Plan()

    P.poolc(lambda g: g.memset(ONESR[:, :], 1.0))
    P.poolc(lambda g: g.memset(ONESC[:, :], 1.0))
    P.poolc(lambda g: g.memset(ONEF[:, :], 1.0))
    import math
    P.poolc(lambda g: g.memset(SHIFT_SB[:, :], -SHIFT))
    P.poolc(lambda g: g.memset(DEPS_SB[:, :], 2048.0 * EPS))
    P.poolc(lambda g: g.memset(LND2_SB[:, :], 0.5 * math.log(2048.0)))
    P.poolc(lambda g: g.memset(NH_SB[:, :], -0.5))
    P.poolc(lambda g: g.memset(IDB[:, :], 0.0))
    v_memset = P.poolc(lambda g: g.affine_select(
        out=IDB[:, :], in_=IDB[:, :], compare_op=ALU.not_equal,
        fill=1.0, base=0, pattern=[[-1, 128]], channel_multiplier=1))

    def chunked(dst_sb, src_dram):
        return lambda g: g.dma_start(
            out=dst_sb[:, :].rearrange("p (c w) -> p c w", c=16),
            in_=src_dram.rearrange("(c p) w -> p c w", c=16))

    P.pdma(chunked(WQ, wqT), "dW")
    P.pdma(chunked(WK, wkT), "dW")
    P.pdma(chunked(WR, wrT), "dW")
    P.pdma(chunked(AT, aT), "dW")
    P.pdma(lambda g: g.dma_start(out=BT[:, :], in_=bTc), "dW")
    P.pdma(lambda g: g.dma_start(out=BB[:, :], in_=bbr), "dW")
    v_w = 0
    for c4 in range(4):
        v_w = P.pdma(lambda g, c4=c4: g.dma_start(
            out=WVT[:, :].rearrange("p (c w) -> p c w", c=16)[:, 4 * c4:4 * c4 + 4, :],
            in_=wvT_f.rearrange("(c p) w -> p c w", c=16)[:, 4 * c4:4 * c4 + 4, :]), "dW")

    groups = [[0, 1, 2, 3], [4, 5, 6, 7]]
    xtt_load_done = {}
    xtt_consumed = {}
    evac_done = {}
    vown_store_done = {}

    def emit_xtt_load(tb):
        w = [("pe", xtt_consumed[tb - 4])] if tb >= 4 else []
        xtt_load_done[tb] = P.pdma(
            lambda g, tb=tb: g.dma_start(
                out=XTT[tb % 4][:, :].rearrange("p (c t) -> p c t", c=16),
                in_=xT.rearrange("(c p) t -> p c t", c=16)[:, :, tb * 128:(tb + 1) * 128]),
            f"dX{tb % 4}", waits=w)

    for _tb in range(4):
        emit_xtt_load(_tb)

    qe_val = {}
    for tb in range(NTB):
        buf = XTT[tb % 4]
        w = [(f"dX{tb % 4}", xtt_load_done[tb]), ("dW", v_w), ("pool", v_memset)]
        if tb >= 1:
            w.append(("dve", qe_val[tb - 1]))
        first = True
        for dc in range(16):
            tile = buf[:, dc * 128:(dc + 1) * 128]
            P.pe(lambda t, tile=tile, dc=dc: t.matmul(
                PSS[:, 0:64], tile, WQ[:, dc * 64:(dc + 1) * 64],
                start=(dc == 0), stop=(dc == 15)),
                waits=w if first else ())
            first = False
            P.pe(lambda t, tile=tile, dc=dc: t.matmul(
                PSL[:, 0:64], tile, WK[:, dc * 64:(dc + 1) * 64],
                start=(dc == 0), stop=(dc == 15)))
            for oc in range(4):
                P.pe(lambda t, tile=tile, oc=oc, dc=dc: t.matmul(
                    PSA[oc][:, 0:512], tile,
                    WVT[:, dc * 2048 + oc * 512: dc * 2048 + (oc + 1) * 512],
                    start=(dc == 0), stop=(dc == 15)))
        pe_mm = P.cnt["pe"]
        xtt_consumed[tb] = pe_mm
        if tb + 4 < NTB:
            emit_xtt_load(tb + 4)

        if tb >= 1:
            tpd = P.pe(lambda t: t.transpose(PST[0][0:64, 0:128], QTMP[:, 0:64],
                                             IDB[:, :]),
                       waits=[("dve", qe)])
            evac_done[tb - 1] = P.dve(lambda v, tb=tb: v.tensor_copy(
                QFT[:, (tb - 1) * 128: tb * 128], PST[0][0:64, 0:128]),
                waits=[("pe", tpd)])
        w2 = [("pe", pe_mm)]
        if tb >= 2:
            w2.append((f"dVO{tb % 2}", vown_store_done[tb - 2]))
        for oc in range(4):
            P.dve(lambda v, tb=tb, oc=oc: v.tensor_copy(
                VOWN[tb % 2][:, oc * 512:(oc + 1) * 512], PSA[oc][:, 0:512]),
                waits=w2 if oc == 0 else ())
        P.dve(lambda v, tb=tb: v.tensor_copy(VOWN[tb % 2][:, 2048:2112], PSL[:, 0:64]))
        qe = qe_val[tb] = P.dve(lambda v: v.tensor_copy(QTMP[:, 0:64], PSS[:, 0:64]))
        vown_store_done[tb] = P.sdma(
            lambda s, tb=tb: s.dma_start(
                out=agv_in.ap()[tb * 128:(tb + 1) * 128, :], in_=VOWN[tb % 2][:, :]),
            f"dVO{tb % 2}", waits=[("dve", qe)])

        P.cc(lambda g, tb=tb: g.collective_compute(
            "AllGather", ALU.bypass, replica_groups=groups,
            ins=[agv_in.ap()[tb * 128:(tb + 1) * 128, :]], outs=[agv_out[tb].ap()]),
            f"agv{tb}", waits=[(f"dVO{tb % 2}", vown_store_done[tb])])

    tpd = P.pe(lambda t: t.transpose(PST[0][0:64, 0:128], QTMP[:, 0:64], IDB[:, :]),
               waits=[("dve", qe_val[NTB - 1])])
    evac_done[NTB - 1] = P.dve(lambda v: v.tensor_copy(
        QFT[:, (NTB - 1) * 128: NTB * 128], PST[0][0:64, 0:128]),
        waits=[("pe", tpd)])


    kft_evac = {}
    ktmp_loaded = {}

    def emit_kft_load(k):
        for i in range(4):
            t = 4 * k + i
            ktmp_loaded[(k, i)] = P.sdma(
                lambda s, k=k, i=i, t=t: s.dma_start(
                    out=KTMP[:, 64 * (16 * k + 4 * i): 64 * (16 * k + 4 * i + 4)]
                    .rearrange("p (n w) -> p n w", n=4),
                    in_=agv_out[t].ap().rearrange("(n p) w -> p n w", n=4)[:, :, 2048:2112]),
                f"dKT{k}", waits=[(f"agv{t}", 1)])

    def emit_kft_chunk(k):
        for n in range(16):
            jj = 16 * k + n
            w = [(f"dKT{k}", ktmp_loaded[(k, 3)])]
            if jj >= 2:
                w.append(("dve", kft_evac[jj - 2]))
            if jj == 0:
                w.append(("dve", evac_done[NTB - 1]))
            tp = P.pe(lambda t, jj=jj: t.transpose(
                PST[jj % 2][0:64, 0:128], KTMP[:, jj * 64:(jj + 1) * 64], IDB[:, :]),
                waits=w)
            kft_evac[jj] = P.dve(lambda v, jj=jj: v.tensor_copy(
                KFT[:, jj * 128:(jj + 1) * 128], PST[jj % 2][0:64, 0:128]),
                waits=[("pe", tp)])

    emit_kft_load(0)
    emit_kft_chunk(0)

    def emit_ws_load(oc, wsrc, extra=()):
        return P.pdma(lambda g, oc=oc, wsrc=wsrc: g.dma_start(
            out=WS[oc % 2][:, :].rearrange("p (c w) -> p c w", c=16),
            in_=wsrc.rearrange("(c p) w -> p c w", c=16)[:, :, oc * 512:(oc + 1) * 512]),
            f"dWS{oc % 2}", waits=list(extra))

    out_store_total = {}
    ph45_pe_done = 0
    vload_consumed_prev = None
    sq_done = {}
    prev_stt_gate = 0
    pf_cur = {}

    for p in range(NP):
        dve_barrier = prev_stt_gate if p >= 1 else P.cnt["dve"]
        pass_pe_barrier = ph45_pe_done
        exp_done = {}
        st_done = {}
        for jj_, v_ in pf_cur.items():
            st_done[jj_], exp_done[jj_] = v_
        pf_next = {}
        acc_evac = {}
        vload = {}
        vload_consumed = {}
        sweep_mid_pe = {}
        sweep_end_pe = {}

        for s_ in range(4):
            def emit_vload(bi, s_=s_, p=p):
                w = [(f"agv{bi}", 1)]
                if bi >= 4:
                    w.append(("pe", vload_consumed[(s_, bi - 4)]))
                elif s_ >= 1:
                    w.append(("pe", vload_consumed[(s_ - 1, 12 + bi)]))
                elif p >= 1:
                    w.append(("pe", vload_consumed_prev[(3, 12 + bi)]))
                emit = P.sdma if (bi % 2 == 0 or (s_ == 0 and bi < 4)) else P.pdma
                vload[(s_, bi)] = emit(
                    lambda e, bi=bi, s_=s_: e.dma_start(
                        out=VS[:, :].rearrange("p (n w) -> p n w", n=16)[
                            :, (bi % 4) * 4:(bi % 4) * 4 + 4, :],
                        in_=agv_out[bi].ap().rearrange("(n p) w -> p n w", n=4)[
                            :, :, s_ * 512:(s_ + 1) * 512]),
                    f"dVS{bi % 4}", waits=w)

            for bb_ in range(4):
                emit_vload(bb_)

            if s_ == 0:
                def st_bank(jj, p=p):
                    if p == 0 or jj % 2 == 0:
                        return PSS[:, 0:512]
                    return PST[0][:, :].bitcast(F32)[:, 0:512]
                if 0 not in st_done:
                    w0 = [("dve", kft_evac[0]),
                          ("dve", dve_barrier), ("pe", pass_pe_barrier)]
                    st_done[0] = P.pe(lambda t, p=p: t.matmul(
                        st_bank(0), KFT[:, 0:128],
                        QFT[:, p * 512:(p + 1) * 512], start=True, stop=True),
                        waits=w0)

            for bi in range(16):
                if p == 0 and s_ == 1 and bi == 1:
                    wsp_load = P.pdma(lambda g: g.dma_start(
                        out=WSP[:, :].rearrange("p (c w) -> p c w", c=16),
                        in_=woT_f.rearrange("(c p) w -> p c w", c=16)[:, :, 0:512]),
                        "dWSP", waits=[("pe", xtt_consumed[NTB - 1])])
                if s_ == 1 and bi == 4:
                    lcol_v = 0
                    for qb in range(4):
                        lcol_v = P.pe(lambda t, qb=qb: t.matmul(
                            PSS[:, qb:qb + 1], LROW[0:1, qb * 128:(qb + 1) * 128],
                            ONEF[0:1, 0:1], start=True, stop=True),
                            waits=[("dve", lrow_v)])
                    P.dve(lambda v: v.tensor_copy(LC[:, 0:4], PSS[:, 0:4]),
                          waits=[("pe", lcol_v)])
                    P.dve(lambda v: v.reciprocal(RL[:, 0:4], LC[:, 0:4]))
                if p == 0 and s_ == 0 and bi >= 4 and bi % 4 == 0:
                    k = bi // 4
                    emit_kft_load(k)
                    emit_kft_chunk(k)
                    jj0 = 16 * k
                    st_done[jj0] = P.pe(lambda t, jj0=jj0, p=p: t.matmul(
                        PSS[:, 0:512], KFT[:, jj0 * 128:(jj0 + 1) * 128],
                        QFT[:, p * 512:(p + 1) * 512], start=True, stop=True),
                        waits=[("act", exp_done[jj0 - 1]), ("dve", kft_evac[jj0])])
                pv_v = 0
                for j8 in range(4):
                    jj = 4 * bi + j8
                    slot = (bi % 4) * 4 + j8
                    if s_ == 0:
                        if jj not in exp_done:
                            exp_done[jj] = P.act(lambda s, jj=jj: s.activation(
                                PT[:, jj * 512:(jj + 1) * 512], st_bank(jj), AF.Exp,
                                bias=SHIFT_SB[:, 0:1]),
                                waits=[("pe", st_done[jj]), ("pool", v_memset)])
                        nxt = jj + 1
                        defer = (p == 0 and nxt % 16 == 0)
                        if nxt < NJ and not defer and nxt not in st_done:
                            eg = exp_done[jj] if (p == 0) else exp_done.get(nxt - 2, 0)
                            wn = [("act", eg), ("dve", kft_evac[nxt])]
                            if p > 0 and nxt % 2 == 1 and nxt == 1:
                                wn.append(("dve", dve_barrier))
                            st_done[nxt] = P.pe(lambda t, nxt=nxt, p=p: t.matmul(
                                st_bank(nxt), KFT[:, nxt * 128:(nxt + 1) * 128],
                                QFT[:, p * 512:(p + 1) * 512], start=True, stop=True),
                                waits=wn)
                    w = [(f"dVS{bi % 4}", vload[(s_, bi)]), ("act", exp_done[jj])]
                    if jj == 0 and s_ == 0:
                        w.append(("dve", dve_barrier))
                    for i in range(4):
                        wi = w if i == 0 else []
                        if jj == 0 and s_ >= 1:
                            wi = list(wi) + [acc_evac[i]]
                        pv_v = P.pe(lambda t, jj=jj, i=i, slot=slot: t.matmul(
                            PSA[i][:, 0:512],
                            VS[:, slot * 512 + i * 128: slot * 512 + (i + 1) * 128],
                            PT[:, jj * 512:(jj + 1) * 512],
                            start=(jj == 0), stop=(jj == 63)),
                            waits=wi)
                    if s_ == 0:
                        pv_v = P.pe(lambda t, jj=jj: t.matmul(
                            PSL[0:1, 0:512], ONESC[:, 0:1],
                            PT[:, jj * 512:(jj + 1) * 512],
                            start=(jj == 0), stop=(jj == 63)))
                    if jj == 31:
                        sweep_mid_pe[s_] = pv_v
                vload_consumed[(s_, bi)] = pv_v
                if bi + 4 < 16:
                    emit_vload(bi + 4)
            sweep_end_pe[s_] = pv_v

            ev = {}
            for i in range(4):
                dcg = 4 * s_ + i
                if i % 2 == 0:
                    ev[i] = ("dve", P.dve(lambda v, i=i, dcg=dcg: v.tensor_copy(
                        OTBUF[:, dcg * 512:(dcg + 1) * 512], PSA[i][:, 0:512]),
                        waits=[("pe", pv_v)] if i == 0 else ()))
                else:
                    ev[i] = ("act", P.act(lambda s, i=i, dcg=dcg: s.activation(
                        OTBUF[:, dcg * 512:(dcg + 1) * 512], PSA[i][:, 0:512],
                        AF.Identity),
                        waits=[("pe", pv_v)] if i == 1 else ()))
            acc_evac = ev
            if s_ == 0:
                lrow_v = P.dve(lambda v: v.tensor_copy(LROW[0:1, 0:512], PSL[0:1, 0:512]),
                               waits=[("pe", pv_v)])
        vload_consumed_prev = vload_consumed
        ot_waits = list(acc_evac.values())

        def stats_dve(tb):
            P.dve(lambda v, tb=tb: v.reduce_sum(
                MEAN4[:, tb:tb + 1], MEANP[:, tb * 4:tb * 4 + 4], axis=AX.X))
            P.dve(lambda v, tb=tb: v.tensor_scalar(
                VAR4[:, tb:tb + 1], MEAN4[:, tb:tb + 1], MEAN4[:, tb:tb + 1],
                -INV_D, ALU.mult, ALU.mult))
            P.dve(lambda v, tb=tb: v.tensor_scalar_add(
                VAR4[:, tb:tb + 1], VAR4[:, tb:tb + 1], 2048.0 * EPS))
            return P.dve(lambda v, tb=tb: v.tensor_scalar_mul(
                MNV4[:, tb:tb + 1], MEAN4[:, tb:tb + 1], INV_D))

        def stats_act(tb, dvegate):
            P.act(lambda s, tb=tb: s.activation(
                SQS[:, 0:4], SQP[:, tb * 4:tb * 4 + 4], AF.Identity,
                accum_out=MSQ4[:, tb:tb + 1]), waits=[("dve", dvegate)])
            P.act(lambda s, tb=tb: s.activation(
                VAR4[:, tb:tb + 1], MSQ4[:, tb:tb + 1], AF.Ln,
                bias=VAR4[:, tb:tb + 1]))
            return P.act(lambda s, tb=tb: s.activation(
                RSTD4[:, tb:tb + 1], VAR4[:, tb:tb + 1], AF.Exp,
                scale=NH_SB[:, 0:1], bias=LND2_SB[:, 0:1]))

        def emit_nm(tb, rs):
            return P.dve(lambda v, tb=tb: v.tensor_scalar(
                NMR4[:, tb:tb + 1], RSTD4[:, tb:tb + 1], MEAN4[:, tb:tb + 1],
                -INV_D, ALU.mult, ALU.mult), waits=[("act", rs)])

        def emit_norm_act(tb, extra=()):
            return P.act(lambda s, tb=tb: s.activation(
                XF32[:, tb * 2048:(tb + 1) * 2048],
                XF32[:, tb * 2048:(tb + 1) * 2048], AF.Identity,
                scale=RSTD4[:, tb:tb + 1], bias=NMR4[:, tb:tb + 1]),
                waits=list(extra))

        def emit_cast(tb, extra=()):
            return P.act(lambda s, tb=tb: s.activation(
                X1B[tb % 2][:, 0:2048], XF32[:, tb * 2048:(tb + 1) * 2048], AF.Identity,
                scale=RSTD4[:, tb:tb + 1], bias=NMR4[:, tb:tb + 1]),
                waits=list(extra))

        def emit_norm(tb, extra=()):
            return P.dve(lambda v, tb=tb: v.tensor_scalar(
                XF32[:, tb * 2048:(tb + 1) * 2048], XF32[:, tb * 2048:(tb + 1) * 2048],
                MNV4[:, tb:tb + 1], RSTD4[:, tb:tb + 1], ALU.subtract, ALU.mult),
                waits=list(extra))

        xf_total = 0
        for t4 in range(4):
            w = [("dOUT", out_store_total[p - 1])] if p >= 1 else []
            xf_total = P.sdma(
                lambda s, p=p, t4=t4: s.dma_start(
                    out=XF32[:, t4 * 2048:(t4 + 1) * 2048],
                    in_=x_tok[p * 512 + t4 * 128: p * 512 + (t4 + 1) * 128, :]),
                "dXF", waits=w)
        stt_done = {}
        ln1_norm = {}
        ln1_nm = {}
        cast_done = {}
        rs1 = {}
        ws_load = {1: emit_ws_load(1, woT_f, [("pe", vload_consumed[(3, 7)])]),
                   2: emit_ws_load(2, woT_f, [("pe", vload_consumed[(3, 3)])])}
        for oc in range(4):
            wsrc = WSP if oc == 0 else WS[oc % 2]
            for tb in range(4):
                if oc == 0:
                    w = [("dWSP", 16)] + ot_waits
                else:
                    w = [(f"dWS{oc % 2}", ws_load[oc])] + ot_waits
                    w.append(("dve", stt_done[(oc - 1, tb)]))
                last_mm = 0
                for dc in range(16):
                    last_mm = P.pe(lambda t, tb=tb, dc=dc, oc=oc, wsrc=wsrc: t.matmul(
                        PSA[tb][:, 0:512],
                        OTBUF[:, dc * 512 + tb * 128: dc * 512 + (tb + 1) * 128],
                        wsrc[:, dc * 512:(dc + 1) * 512],
                        start=(dc == 0), stop=(dc == 15)),
                        waits=w if dc == 0 else ())
                sl = slice(tb * 2048 + oc * 512, tb * 2048 + (oc + 1) * 512)
                stt_done[(oc, tb)] = P.dve(
                    lambda v, sl=sl, tb=tb, oc=oc: v.scalar_tensor_tensor(
                        XF32[:, sl], PSA[tb][:, 0:512], RL[:, tb:tb + 1], XF32[:, sl],
                        ALU.mult, ALU.add,
                        accum_out=MEANP[:, tb * 4 + oc: tb * 4 + oc + 1]),
                    waits=[("pe", last_mm), ("dXF", xf_total)])
                sq_done[(oc, tb)] = P.act(
                    lambda s, sl=sl, tb=tb, oc=oc: s.activation(
                        SQS[:, 0:512], XF32[:, sl], AF.Square,
                        accum_out=SQP[:, tb * 4 + oc: tb * 4 + oc + 1]),
                    waits=[("dve", stt_done[(oc, tb)])])
                if oc == 3:
                    rs1[tb] = stats_act(tb, stats_dve(tb))
                    if tb <= 1:
                        ln1_nm[tb] = emit_nm(tb, rs1[tb])
                        cast_done[tb] = emit_cast(tb, [("dve", ln1_nm[tb])])
                        ln1_norm[tb] = emit_norm_act(tb)
            if oc == 1:
                ws_load[3] = emit_ws_load(3, woT_f, [("pe", P.cnt["pe"])])

        ph4_pe = P.cnt["pe"]
        ws5_gate = ph4_pe
        x1t_evac = {}
        route_done = {}
        hp_done = {}
        ht_pe_val = {}

        def emit_ht(tbh):
            ht_pe_val[tbh] = P.pe(lambda t: t.transpose(
                PSL[:, :].bitcast(BF16)[0:64, 0:128], HP[:, 0:64], IDB[:, :]),
                waits=[("dve", hp_done[tbh])])
            route_done[tbh] = P.dve(lambda v, tbh=tbh: v.tensor_copy(
                HT[:, tbh * 128:(tbh + 1) * 128],
                PSL[:, :].bitcast(BF16)[0:64, 0:128]),
                waits=[("pe", ht_pe_val[tbh])])

        def x1t_slot(i):
            m = i % 4
            if m < 2:
                return PST[m][:, 0:128]
            return PSA[m - 2][:, :].bitcast(BF16)[:, 0:128]

        for tb in range(4):
            if tb <= 1:
                cast_v = cast_done[tb]
            else:
                ln1_nm[tb] = emit_nm(tb, rs1[tb])
                cast_v = cast_done[tb] = emit_cast(
                    tb, [("dve", ln1_nm[tb]),
                         x1t_evac[(tb - 2) * 16 + 15]])
                ln1_norm[tb] = emit_norm_act(tb)
            for dc in range(16):
                i = tb * 16 + dc
                w = [("act", cast_v), ("pe", ph4_pe)]
                if i >= 4:
                    w.append(x1t_evac[i - 4])
                if i in (2, 3):
                    w.append(("dve", stt_done[(3, i - 2)]))
                P.pe(lambda t, tb=tb, dc=dc, i=i: t.transpose(
                    x1t_slot(i), X1B[tb % 2][:, dc * 128:(dc + 1) * 128],
                    IDB[:, :]),
                    waits=w)
                if i % 4 < 2:
                    x1t_evac[i] = ("act", P.act(
                        lambda s, i=i: s.activation(
                            TPOSE[:, i * 128:(i + 1) * 128], x1t_slot(i),
                            AF.Identity),
                        waits=[("pe", P.cnt["pe"])]))
                else:
                    x1t_evac[i] = ("dve", P.dve(
                        lambda v, i=i: v.tensor_copy(
                            TPOSE[:, i * 128:(i + 1) * 128], x1t_slot(i)),
                        waits=[("pe", P.cnt["pe"])]))
            if tb >= 1:
                emit_ht(tb - 1)
            lg_v = 0
            for dc in range(16):
                lg_v = P.pe(lambda t, tb=tb, dc=dc: t.matmul(
                    PSS[:, 0:8], TPOSE[:, (tb * 16 + dc) * 128:(tb * 16 + dc + 1) * 128],
                    WR[:, dc * 8:(dc + 1) * 8], start=(dc == 0), stop=(dc == 15)),
                    waits=[x1t_evac[tb * 16 + dc]])
            h_v = 0
            for dc in range(16):
                wh = []
                if dc == 0 and tb >= 1:
                    wh.append(("dve", route_done[tb - 1]))
                h_v = P.pe(lambda t, tb=tb, dc=dc: t.matmul(
                    PSL[:, 0:64], TPOSE[:, (tb * 16 + dc) * 128:(tb * 16 + dc + 1) * 128],
                    AT[:, dc * 64:(dc + 1) * 64], start=(dc == 0), stop=(dc == 15)),
                    waits=wh)
            lg_copy = P.dve(lambda v: v.tensor_copy(LG[:, :], PSS[:, 0:8]),
                            waits=[("pe", lg_v)])
            P.dve(lambda v: v.max(T8[:, :], LG[:, :]))
            P.dve(lambda v: v.tensor_scalar_mul(NM1[:, 0:1], T8[:, 0:1], -1.0))
            P.dve(lambda v: v.tensor_scalar(M2B[:, :], LG[:, :], 0.0, T8[:, 1:2],
                                            ALU.mult, ALU.add))
            msk_v = P.dve(lambda v: v.tensor_tensor(MSK[:, :], LG[:, :], M2B[:, :], ALU.is_ge))
            eg_v = P.act(lambda s: s.activation(EG[:, :], LG[:, :], AF.Exp,
                                                bias=NM1[:, 0:1]), waits=[("dve", msk_v)])
            e2_v = P.act(lambda s: s.activation(DEN[:, 0:1], T8[:, 1:2], AF.Exp,
                                                bias=NM1[:, 0:1]))
            P.dve(lambda v: v.tensor_scalar(DEN[:, 0:1], DEN[:, 0:1], 1.0, 0.5,
                                            ALU.add, ALU.mult), waits=[("act", e2_v)])
            P.dve(lambda v: v.reciprocal(RD[:, 0:1], DEN[:, 0:1]))
            P.dve(lambda v: v.scalar_tensor_tensor(WG[:, :], EG[:, :], RD[:, 0:1],
                                                   MSK[:, :], ALU.mult, ALU.mult),
                  waits=[("act", eg_v)])
            P.dve(lambda v: v.tensor_copy(H[:, :], PSL[:, 0:64]), waits=[("pe", h_v)])
            hp_v = 0
            for e in range(8):
                wh = [("pe", ht_pe_val[tb - 1])] if (e == 0 and tb >= 1) else ()
                hp_v = P.dve(lambda v, e=e: v.tensor_scalar_mul(
                    HP[:, e * 8:(e + 1) * 8], H[:, e * 8:(e + 1) * 8], WG[:, e:e + 1]),
                    waits=wh)
            hp_done[tb] = hp_v
            if tb == 3:
                emit_ht(3)

        ws_load5 = {0: emit_ws_load(0, wbT_f, [("pe", ws5_gate)]),
                    1: emit_ws_load(1, wbT_f, [("pe", ws5_gate)])}
        stt2 = {}
        ln2_norm = {}
        rs2 = {}
        for oc in range(4):
            for tb in range(4):
                w = [(f"dWS{oc % 2}", ws_load5[oc]), ("dve", route_done[tb])]
                if oc >= 1:
                    w.append(("dve", stt2[(oc - 1, tb)]))
                elif tb <= 1:
                    w.append(x1t_evac[62 + tb])
                for dc in range(16):
                    P.pe(lambda t, tb=tb, dc=dc, oc=oc: t.matmul(
                        PSA[tb][:, 0:512],
                        TPOSE[:, (tb * 16 + dc) * 128:(tb * 16 + dc + 1) * 128],
                        WS[oc % 2][:, dc * 512:(dc + 1) * 512],
                        start=(dc == 0), stop=False),
                        waits=w if dc == 0 else ())
                P.pe(lambda t, oc=oc, tb=tb: t.matmul(
                    PSA[tb][:, 0:512], ONESR[0:1, 0:128],
                    BB[0:1, oc * 512:(oc + 1) * 512], start=False, stop=False))
                last_mm = P.pe(lambda t, oc=oc, tb=tb: t.matmul(
                    PSA[tb][:, 0:512], HT[:, tb * 128:(tb + 1) * 128],
                    BT[:, oc * 512:(oc + 1) * 512], start=False, stop=True))
                sl = slice(tb * 2048 + oc * 512, tb * 2048 + (oc + 1) * 512)
                wstt = [("pe", last_mm)]
                if oc == 0:
                    wstt.append(("act", ln1_norm[tb]))
                stt2[(oc, tb)] = P.dve(
                    lambda v, sl=sl, tb=tb, oc=oc: v.scalar_tensor_tensor(
                        XF32[:, sl], PSA[tb][:, 0:512], 1.0, XF32[:, sl],
                        ALU.mult, ALU.add,
                        accum_out=MEANP[:, tb * 4 + oc: tb * 4 + oc + 1]),
                    waits=wstt)
                sq_done[(oc, tb)] = P.act(
                    lambda s, sl=sl, tb=tb, oc=oc: s.activation(
                        SQS[:, 0:512], XF32[:, sl], AF.Square,
                        accum_out=SQP[:, tb * 4 + oc: tb * 4 + oc + 1]),
                    waits=[("dve", stt2[(oc, tb)])])
                if oc == 3:
                    rs2[tb] = stats_act(tb, stats_dve(tb))
                if oc == 3 and tb == 1 and p + 1 < NP:
                    st2 = P.pe(lambda t, p=p: t.matmul(
                        PSS[:, 0:512], KFT[:, 256:384],
                        QFT[:, (p + 1) * 512:(p + 2) * 512],
                        start=True, stop=True),
                        waits=[("act", pf_next[0][1])])
                    e2 = P.act(lambda s: s.activation(
                        PT[:, 1024:1536], PSS[:, 0:512], AF.Exp,
                        bias=SHIFT_SB[:, 0:1]), waits=[("pe", st2)])
                    st3 = P.pe(lambda t, p=p: t.matmul(
                        PST[0][:, :].bitcast(F32)[:, 0:512], KFT[:, 384:512],
                        QFT[:, (p + 1) * 512:(p + 2) * 512],
                        start=True, stop=True),
                        waits=[("act", pf_next[1][1])])
                    e3 = P.act(lambda s: s.activation(
                        PT[:, 1536:2048], PST[0][:, :].bitcast(F32)[:, 0:512],
                        AF.Exp, bias=SHIFT_SB[:, 0:1]), waits=[("pe", st3)])
                    pf_next[2] = (st2, e2)
                    pf_next[3] = (st3, e3)
            if oc + 2 < 4:
                ws_load5[oc + 2] = emit_ws_load(oc + 2, wbT_f, [("pe", P.cnt["pe"])])
            if oc == 2 and p + 1 < NP:
                st0 = P.pe(lambda t, p=p: t.matmul(
                    PSS[:, 0:512], KFT[:, 0:128],
                    QFT[:, (p + 1) * 512:(p + 2) * 512], start=True, stop=True),
                    waits=[("dve", lg_copy)])
                e0 = P.act(lambda s: s.activation(
                    PT[:, 0:512], PSS[:, 0:512], AF.Exp, bias=SHIFT_SB[:, 0:1]),
                    waits=[("pe", st0)])
                st1 = P.pe(lambda t, p=p: t.matmul(
                    PST[0][:, :].bitcast(F32)[:, 0:512], KFT[:, 128:256],
                    QFT[:, (p + 1) * 512:(p + 2) * 512], start=True, stop=True),
                    waits=[x1t_evac[60]])
                e1 = P.act(lambda s: s.activation(
                    PT[:, 512:1024], PST[0][:, :].bitcast(F32)[:, 0:512],
                    AF.Exp, bias=SHIFT_SB[:, 0:1]),
                    waits=[("pe", st1)])
                pf_next = {0: (st0, e0), 1: (st1, e1)}
        prev_stt_gate = stt2[(3, 3)]
        for tb in range(4):
            ln2_norm[tb] = emit_norm(tb, [("act", rs2[tb])])

        tot = 0
        for tb in range(4):
            tot = P.pdma(
                lambda g, p=p, tb=tb: g.dma_start(
                    out=y_out[p * 512 + tb * 128: p * 512 + (tb + 1) * 128, :],
                    in_=XF32[:, tb * 2048:(tb + 1) * 2048]),
                "dOUT", waits=[("dve", ln2_norm[tb])])
        out_store_total[p] = tot
        ph45_pe_done = P.cnt["pe"]
        pf_cur = pf_next

    P.wait_only("pool", [("dOUT", P.cnt["dOUT"])])

    with contextlib.ExitStack() as stack:
        sems = {}
        for name in ["pe", "dve", "act", "pool"] + P.dma_sems + P.cc_sems:
            sems[name] = stack.enter_context(nc.semaphore(f"sem_{name}"))
        block = stack.enter_context(nc.Block())

        def run_stream(eng_obj, entries):
            last_waited = {}
            for waits, fn, inc in entries:
                for sname, val in waits:
                    if val <= 0 or last_waited.get(sname, -1) >= val:
                        continue
                    eng_obj.wait_ge(sems[sname], val)
                    last_waited[sname] = val
                if fn is None:
                    continue
                inst = fn(eng_obj)
                if inc is not None:
                    inst.then_inc(sems[inc[0]], inc[1])

        @block.tensor
        def _(t):
            run_stream(t, P.ops["pe"])

        @block.vector
        def _(v):
            run_stream(v, P.ops["dve"])

        @block.scalar
        def _(s):
            run_stream(s, P.ops["act"])

        @block.gpsimd
        def _(g):
            run_stream(g, P.ops["pool"])

        @block.sync
        def _(s):
            run_stream(s, P.ops["sync"])

    return nc


_NC_CACHE = {}


def _get_nc():
    if "nc" not in _NC_CACHE:
        _NC_CACHE["nc"] = build()
    return _NC_CACHE["nc"]


def kernel(L0_attn_hidden_states, L0_attn_Wq, L0_attn_Wk, L0_attn_Wv, L0_attn_Wo,
           L0_ffn_W_base, L0_ffn_b_base, L0_ffn_W_router, L0_ffn_A_w, L0_ffn_B_w):
    x = np.ascontiguousarray(np.asarray(L0_attn_hidden_states, dtype=np.float32)).reshape(16384, 2048)
    wq = np.asarray(L0_attn_Wq, dtype=np.float32)
    wk = np.asarray(L0_attn_Wk, dtype=np.float32)
    wv = np.asarray(L0_attn_Wv, dtype=np.float32)
    wo = np.asarray(L0_attn_Wo, dtype=np.float32)
    wb = np.asarray(L0_ffn_W_base, dtype=np.float32)
    bb = np.asarray(L0_ffn_b_base, dtype=np.float32)
    wr = np.asarray(L0_ffn_W_router, dtype=np.float32)
    aw = np.asarray(L0_ffn_A_w, dtype=np.float32)
    bw = np.asarray(L0_ffn_B_w, dtype=np.float32)

    shared = {
        "wqT": np.ascontiguousarray(wq.T),
        "wkT": np.ascontiguousarray(wk.T),
        "wvT_f": np.ascontiguousarray(wv.T),
        "woT_f": np.ascontiguousarray(wo.T),
        "wbT_f": np.ascontiguousarray(wb.T),
        "wrT": np.ascontiguousarray(wr.T),
        "aT": np.ascontiguousarray(aw.transpose(2, 0, 1).reshape(2048, 64)),
        "bTc": np.ascontiguousarray(bw.transpose(0, 2, 1).reshape(64, 2048)),
        "bbr": np.ascontiguousarray(bb.reshape(1, 2048)),
    }

    nc = _get_nc()
    in_maps = []
    for c in range(8):
        sl = x[2048 * c:2048 * (c + 1)]
        m = dict(shared)
        m["x_tok"] = np.ascontiguousarray(sl)
        m["xT"] = np.ascontiguousarray(sl.T)
        in_maps.append(m)

    res = run_bass_kernel_spmd(nc, in_maps, core_ids=list(range(8)))
    out = np.concatenate([np.asarray(res.results[c]["y_out"]) for c in range(8)], axis=0)
    return out.reshape(2, 8192, 2048).astype(np.float32, copy=False)
